# version 02 (final)
# baseline (speedup 1.0000x reference)
import numpy as np
import ml_dtypes
import concourse.bass as bass
from contextlib import ExitStack
import concourse.mybir as mybir
from concourse.bass_utils import run_bass_kernel_spmd

F32 = mybir.dt.float32
BF16 = mybir.dt.bfloat16
AF = mybir.ActivationFunctionType
ALU = mybir.AluOpType
AX = mybir.AxisListType
BF = ml_dtypes.bfloat16

D = 2048
B = 2
N = 8192
T = B * N
NCORE = 8
TPC = T // NCORE
MOE_D = 1408
K0 = 6
HD = 128
EPS_LN = 1e-5
EPS_GN = 1e-6
NEG = -1.0e9


class Sched:
    def __init__(self):
        self.items = []
        self.ct = 0

    def op(self, eng, fn, seq=True, inc=1):
        w = self.ct if seq else None
        self.items.append((eng, fn, w, inc))
        self.ct += inc

    def emit(self, block, c):
        total = self.ct
        for name in ("sync", "tensor", "vector", "scalar", "gpsimd"):
            items = [it for it in self.items if it[0] == name]

            def body(e, items=items, name=name):
                for (_, fn, w, inc) in items:
                    if w is not None:
                        e.wait_ge(c, w)
                    ins = fn(e)
                    if inc:
                        ins.then_inc(c, inc)
                if name == "gpsimd":
                    e.wait_ge(c, total)

            getattr(block, name)(body)


def _loadw(S, dst, src2d, col0, ncols, nkt, grp=4):
    for k0 in range(0, nkt, grp):
        g = min(grp, nkt - k0)
        S.op("sync", lambda e, k0=k0, g=g: e.dma_start(
            out=dst[:, k0 * ncols:(k0 + g) * ncols].rearrange("p (k m) -> p k m", k=g),
            in_=src2d[k0 * 128:(k0 + g) * 128, col0:col0 + ncols].rearrange("(k p) m -> p k m", p=128)),
            inc=16)


def emit_ln(S, xs_fn, out_fn, pp, bufs, w=512, nkt=16, dmodel=2048):
    (onesc, ones1, m_sb, v_sb, r_sb, sq_sb, a_sb, epsap) = bufs
    for kt in range(nkt):
        S.op("tensor", lambda e, kt=kt: e.matmul(out=pp[0][0:1, 0:w], lhsT=onesc[:], rhs=xs_fn(kt),
             start=(kt == 0), stop=(kt == nkt - 1), skip_group_check=True),
             seq=(kt == 0), inc=(1 if kt == nkt - 1 else 0))
    for kt in range(nkt):
        S.op("scalar", lambda e, kt=kt: e.activation(out=sq_sb[:, 0:w], in_=xs_fn(kt), func=AF.Square))
        S.op("tensor", lambda e, kt=kt: e.matmul(out=pp[1][0:1, 0:w], lhsT=onesc[:], rhs=sq_sb[:, 0:w],
             start=(kt == 0), stop=(kt == nkt - 1), skip_group_check=True))
    S.op("scalar", lambda e: e.activation(out=m_sb[0:1, 0:w], in_=pp[0][0:1, 0:w], func=AF.Copy, scale=1.0 / dmodel))
    S.op("scalar", lambda e: e.activation(out=v_sb[0:1, 0:w], in_=pp[1][0:1, 0:w], func=AF.Copy, scale=1.0 / dmodel))
    S.op("scalar", lambda e: e.activation(out=r_sb[0:1, 0:w], in_=m_sb[0:1, 0:w], func=AF.Square))
    S.op("vector", lambda e: e.tensor_tensor(out=v_sb[0:1, 0:w], in0=v_sb[0:1, 0:w], in1=r_sb[0:1, 0:w], op=ALU.subtract))
    S.op("scalar", lambda e: e.activation(out=v_sb[0:1, 0:w], in_=v_sb[0:1, 0:w], func=AF.Sqrt, bias=epsap[0:1, 0:1]))
    S.op("vector", lambda e: e.reciprocal(out=r_sb[0:1, 0:w], in_=v_sb[0:1, 0:w]))
    S.op("tensor", lambda e: e.matmul(out=pp[2][:, 0:w], lhsT=ones1[:], rhs=m_sb[0:1, 0:w], start=True, stop=True, skip_group_check=True))
    S.op("tensor", lambda e: e.matmul(out=pp[3][:, 0:w], lhsT=ones1[:], rhs=r_sb[0:1, 0:w], start=True, stop=True, skip_group_check=True))
    for kt in range(nkt):
        S.op("vector", lambda e, kt=kt: e.tensor_tensor(out=a_sb[:, 0:w], in0=xs_fn(kt), in1=pp[2][:, 0:w], op=ALU.subtract))
        S.op("vector", lambda e, kt=kt: e.tensor_tensor(out=out_fn(kt), in0=a_sb[:, 0:w], in1=pp[3][:, 0:w], op=ALU.mult))


def build_l1():
    nc = bass.Bass("TRN2", target_bir_lowering=False)
    xTs = nc.dram_tensor("xTs", [D, TPC], F32, kind="ExternalInput").ap()
    wq = nc.dram_tensor("wq", [D, D], F32, kind="ExternalInput").ap()
    wk = nc.dram_tensor("wk", [D, 512], F32, kind="ExternalInput").ap()
    wv = nc.dram_tensor("wv", [D, 512], F32, kind="ExternalInput").ap()
    cosT = nc.dram_tensor("cosT", [128, TPC], F32, kind="ExternalInput").ap()
    sinT = nc.dram_tensor("sinT", [128, TPC], F32, kind="ExternalInput").ap()
    epsd = nc.dram_tensor("epsd", [1, 1], F32, kind="ExternalInput").ap()
    xnb = nc.dram_tensor("xnb", [D, TPC], BF16, kind="ExternalOutput").ap()
    qT = nc.dram_tensor("qT", [16, 128, TPC], BF16, kind="ExternalOutput").ap()
    kT = nc.dram_tensor("kT", [4, 128, TPC], BF16, kind="ExternalOutput").ap()
    vT = nc.dram_tensor("vT", [4, 128, TPC], BF16, kind="ExternalOutput").ap()
    S = Sched()
    with ExitStack() as _es:
        xs = _es.enter_context(nc.sbuf_tensor("xs", [128, 16 * 512], F32))
        xnsb = _es.enter_context(nc.sbuf_tensor("xnsb", [128, 16 * TPC], BF16))
        wst = _es.enter_context(nc.sbuf_tensor("wst", [128, 16 * 128], F32))
        wb = _es.enter_context(nc.sbuf_tensor("wb", [128, 16 * 128], BF16))
        cs = _es.enter_context(nc.sbuf_tensor("cs", [128, TPC], F32))
        sn = _es.enter_context(nc.sbuf_tensor("sn", [128, TPC], F32))
        a_sb = _es.enter_context(nc.sbuf_tensor("a_sb", [128, 512], F32))
        b_sb = _es.enter_context(nc.sbuf_tensor("b_sb", [128, 512], F32))
        qf = _es.enter_context(nc.sbuf_tensor("qf", [128, 512], F32))
        qo = _es.enter_context(nc.sbuf_tensor("qo", [128, 512], BF16))
        sq_sb = _es.enter_context(nc.sbuf_tensor("sq_sb", [128, 512], F32))
        m_sb = _es.enter_context(nc.sbuf_tensor("m_sb", [1, 512], F32))
        v_sb = _es.enter_context(nc.sbuf_tensor("v_sb", [1, 512], F32))
        r_sb = _es.enter_context(nc.sbuf_tensor("r_sb", [1, 512], F32))
        onesc = _es.enter_context(nc.sbuf_tensor("onesc", [128, 1], F32))
        ones1 = _es.enter_context(nc.sbuf_tensor("ones1", [1, 128], F32))
        epsap = _es.enter_context(nc.sbuf_tensor("epsap", [1, 1], F32))
        pp0 = _es.enter_context(nc.psum_tensor("pp0", [128, 512], F32))
        pp1 = _es.enter_context(nc.psum_tensor("pp1", [128, 512], F32))
        pp2 = _es.enter_context(nc.psum_tensor("pp2", [128, 512], F32))
        pp3 = _es.enter_context(nc.psum_tensor("pp3", [128, 512], F32))
        c = _es.enter_context(nc.semaphore("c"))
        block = _es.enter_context(nc.Block())
        pp = [pp0, pp1, pp2, pp3]
        S.op("vector", lambda e: e.memset(onesc[:], 1.0))
        S.op("vector", lambda e: e.memset(ones1[:], 1.0))
        S.op("sync", lambda e: e.dma_start(out=epsap[:], in_=epsd[:, :]), inc=16)
        S.op("sync", lambda e: e.dma_start(out=cs[:], in_=cosT[:, :]), inc=16)
        S.op("sync", lambda e: e.dma_start(out=sn[:], in_=sinT[:, :]), inc=16)
        lnbufs = (onesc, ones1, m_sb, v_sb, r_sb, sq_sb, a_sb, epsap)
        for s4 in range(4):
            for kt in range(16):
                S.op("sync", lambda e, kt=kt, s4=s4: e.dma_start(
                    out=xs[:, kt * 512:(kt + 1) * 512],
                    in_=xTs[kt * 128:(kt + 1) * 128, s4 * 512:(s4 + 1) * 512]), inc=16)
            emit_ln(S, lambda kt: xs[:, kt * 512:(kt + 1) * 512],
                    lambda kt, s4=s4: xnsb[:, kt * TPC + s4 * 512: kt * TPC + (s4 + 1) * 512],
                    pp, lnbufs)
        for kt in range(16):
            S.op("gpsimd", lambda e, kt=kt: e.dma_start(
                out=xnb[kt * 128:(kt + 1) * 128, :], in_=xnsb[:, kt * TPC:(kt + 1) * TPC]), inc=16)

        def proj(wsrc, mcount, outt, rope, scale):
            for m in range(mcount):
                _loadw(S, wst, wsrc, m * 128, 128, 16)
                S.op("scalar", lambda e: e.copy(out=wb[:], in_=wst[:]))
                for s4 in range(4):
                    for kt in range(16):
                        S.op("tensor", lambda e, kt=kt, s4=s4: e.matmul(
                            out=pp0[:, 0:512], lhsT=wb[:, kt * 128:(kt + 1) * 128],
                            rhs=xnsb[:, kt * TPC + s4 * 512: kt * TPC + (s4 + 1) * 512],
                            start=(kt == 0), stop=(kt == 15), skip_group_check=True),
                            seq=(kt == 0), inc=(1 if kt == 15 else 0))
                    if rope:
                        S.op("vector", lambda e, s4=s4: e.tensor_tensor(out=a_sb[:], in0=pp0[:, 0:512], in1=cs[:, s4 * 512:(s4 + 1) * 512], op=ALU.mult))
                        S.op("vector", lambda e, s4=s4: e.tensor_tensor(out=b_sb[:], in0=pp0[:, 0:512], in1=sn[:, s4 * 512:(s4 + 1) * 512], op=ALU.mult))
                        S.op("scalar", lambda e: e.mul(out=qf[0:64, :], in_=b_sb[64:128, :], mul=-1.0))
                        S.op("scalar", lambda e: e.copy(out=qf[64:128, :], in_=b_sb[0:64, :]))
                        S.op("vector", lambda e: e.tensor_tensor(out=qf[:], in0=qf[:], in1=a_sb[:], op=ALU.add))
                        S.op("scalar", lambda e: e.activation(out=qo[:], in_=qf[:], func=AF.Copy, scale=scale))
                    else:
                        S.op("scalar", lambda e: e.activation(out=qo[:], in_=pp0[:, 0:512], func=AF.Copy, scale=scale))
                    S.op("gpsimd", lambda e, m=m, s4=s4: e.dma_start(
                        out=outt[m, :, s4 * 512:(s4 + 1) * 512], in_=qo[:]), inc=16)
        proj(wq, 16, qT, True, 1.0 / float(np.sqrt(128.0)))
        proj(wk, 4, kT, True, 1.0)
        proj(wv, 4, vT, False, 1.0)
        S.emit(block, c)
    return nc


def build_l2():
    nc = bass.Bass("TRN2", target_bir_lowering=False)
    qTd = nc.dram_tensor("qTd", [4, 128, N], BF16, kind="ExternalInput").ap()
    kTd = nc.dram_tensor("kTd", [128, N], BF16, kind="ExternalInput").ap()
    vNd = nc.dram_tensor("vNd", [N, 128], BF16, kind="ExternalInput").ap()
    mkd = nc.dram_tensor("mkd", [128, 128], F32, kind="ExternalInput").ap()
    idd = nc.dram_tensor("idd", [128, 128], BF16, kind="ExternalInput").ap()
    aoT = nc.dram_tensor("aoT", [4, 128, N], BF16, kind="ExternalOutput").ap()
    S = Sched()
    NT = N // 128
    with ExitStack() as _es:
        kts = _es.enter_context(nc.sbuf_tensor("kts", [128, N], BF16))
        vts = _es.enter_context(nc.sbuf_tensor("vts", [128, NT * 128], BF16))
        qts = _es.enter_context(nc.sbuf_tensor("qts", [128, N], BF16))
        srow = _es.enter_context(nc.sbuf_tensor("srow", [128, N], F32))
        pb = _es.enter_context(nc.sbuf_tensor("pb", [128, N], BF16))
        pt_sb = _es.enter_context(nc.sbuf_tensor("pt_sb", [128, 128], BF16))
        ao_sb = _es.enter_context(nc.sbuf_tensor("ao_sb", [128, N], BF16))
        mk = _es.enter_context(nc.sbuf_tensor("mk", [128, 128], F32))
        idb = _es.enter_context(nc.sbuf_tensor("idb", [128, 128], BF16))
        nmax = _es.enter_context(nc.sbuf_tensor("nmax", [128, 1], F32))
        den = _es.enter_context(nc.sbuf_tensor("den", [128, 1], F32))
        rec = _es.enter_context(nc.sbuf_tensor("rec", [128, 1], F32))
        recb = _es.enter_context(nc.sbuf_tensor("recb", [128, 1], BF16))
        ps_s = _es.enter_context(nc.psum_tensor("ps_s", [128, 512], F32))
        ps_t = _es.enter_context(nc.psum_tensor("ps_t", [128, 128], BF16))
        ps_o = _es.enter_context(nc.psum_tensor("ps_o", [128, 128], F32))
        c = _es.enter_context(nc.semaphore("c"))
        block = _es.enter_context(nc.Block())
        S.op("sync", lambda e: e.dma_start(out=mk[:], in_=mkd[:, :]), inc=16)
        S.op("sync", lambda e: e.dma_start(out=idb[:], in_=idd[:, :]), inc=16)
        S.op("sync", lambda e: e.dma_start(out=kts[:], in_=kTd[:, :]), inc=16)
        for t0 in range(0, NT, 8):
            S.op("sync", lambda e, t0=t0: e.dma_start(
                out=vts[:, t0 * 128:(t0 + 8) * 128].rearrange("p (t d) -> p t d", d=128),
                in_=vNd[t0 * 128:(t0 + 8) * 128, :].rearrange("(t p) d -> p t d", p=128)), inc=16)
        for h in range(4):
            S.op("sync", lambda e, h=h: e.dma_start(out=qts[:], in_=qTd[h, :, :]), inc=16)
            for qi in range(NT):
                nk = (qi + 1) * 128
                nkc = (nk + 511) // 512
                for kc in range(nkc):
                    w = min(512, nk - kc * 512)
                    S.op("tensor", lambda e, qi=qi, kc=kc, w=w: e.matmul(
                        out=ps_s[:, 0:w], lhsT=qts[:, qi * 128:(qi + 1) * 128],
                        rhs=kts[:, kc * 512:kc * 512 + w], start=True, stop=True, skip_group_check=True))
                    S.op("vector", lambda e, kc=kc, w=w: e.tensor_copy(
                        out=srow[:, kc * 512:kc * 512 + w], in_=ps_s[:, 0:w]))
                S.op("vector", lambda e, qi=qi: e.tensor_tensor(
                    out=srow[:, qi * 128:(qi + 1) * 128], in0=srow[:, qi * 128:(qi + 1) * 128],
                    in1=mk[:], op=ALU.add))
                S.op("vector", lambda e, nk=nk: e.reduce_max(out=nmax[:], in_=srow[:, 0:nk], axis=AX.X, negate=True))
                S.op("scalar", lambda e, nk=nk: e.activation(
                    out=pb[:, 0:nk], in_=srow[:, 0:nk], func=AF.Exp, bias=nmax[:], scale=1.0, accum_out=den[:]))
                S.op("vector", lambda e: e.reciprocal(out=rec[:], in_=den[:]))
                S.op("vector", lambda e, nk=nk: e.tensor_scalar(
                    out=pb[:, 0:nk], in0=pb[:, 0:nk], scalar1=rec[:], scalar2=None, op0=ALU.mult))
                for kt in range(qi + 1):
                    S.op("tensor", lambda e, kt=kt: e.transpose(
                        out=ps_t[:], in_=pb[:, kt * 128:(kt + 1) * 128], identity=idb[:]))
                    S.op("vector", lambda e: e.tensor_copy(out=pt_sb[:], in_=ps_t[:]))
                    S.op("tensor", lambda e, kt=kt, qi=qi: e.matmul(
                        out=ps_o[:], lhsT=vts[:, kt * 128:(kt + 1) * 128], rhs=pt_sb[:],
                        start=(kt == 0), stop=(kt == qi), skip_group_check=True))
                S.op("scalar", lambda e, qi=qi: e.copy(out=ao_sb[:, qi * 128:(qi + 1) * 128], in_=ps_o[:]))
            S.op("gpsimd", lambda e, h=h: e.dma_start(out=aoT[h, :, :], in_=ao_sb[:]), inc=16)
        S.emit(block, c)
    return nc


def build_l3():
    nc = bass.Bass("TRN2", target_bir_lowering=False)
    xTs = nc.dram_tensor("xTs", [D, TPC], F32, kind="ExternalInput").ap()
    xnd = nc.dram_tensor("xnd", [D, TPC], BF16, kind="ExternalInput").ap()
    aod = nc.dram_tensor("aod", [D, TPC], BF16, kind="ExternalInput").ap()
    wo = nc.dram_tensor("wo", [D, D], F32, kind="ExternalInput").ap()
    gwT = nc.dram_tensor("gwT", [D, 16], F32, kind="ExternalInput").ap()
    wgT = nc.dram_tensor("wgT", [16, D, MOE_D], F32, kind="ExternalInput").ap()
    wuT = nc.dram_tensor("wuT", [16, D, MOE_D], F32, kind="ExternalInput").ap()
    wdT = nc.dram_tensor("wdT", [16, MOE_D, D], F32, kind="ExternalInput").ap()
    seld = nc.dram_tensor("seld", [16, 16 * 128], F32, kind="ExternalInput").ap()
    idd = nc.dram_tensor("idd", [128, 128], F32, kind="ExternalInput").ap()
    epsd = nc.dram_tensor("epsd", [1, 1], F32, kind="ExternalInput").ap()
    x1T = nc.dram_tensor("x1T", [D, TPC], F32, kind="ExternalOutput").ap()
    xn1 = nc.dram_tensor("xn1", [D, TPC], BF16, kind="ExternalOutput").ap()
    S = Sched()
    CH = 512
    with ExitStack() as _es:
        xn_sb = _es.enter_context(nc.sbuf_tensor("xn_sb", [128, 16 * CH], BF16))
        h_sb = _es.enter_context(nc.sbuf_tensor("h_sb", [128, 11 * CH], BF16))
        y_sb = _es.enter_context(nc.sbuf_tensor("y_sb", [128, 16 * CH], F32))
        sT_sb = _es.enter_context(nc.sbuf_tensor("sT_sb", [128, 16 * CH], BF16))
        wst = _es.enter_context(nc.sbuf_tensor("wst", [128, 16 * 128], F32))
        wga = _es.enter_context(nc.sbuf_tensor("wga", [128, 16 * 128], BF16))
        wua = _es.enter_context(nc.sbuf_tensor("wua", [128, 16 * 128], BF16))
        wda = _es.enter_context(nc.sbuf_tensor("wda", [128, 11 * 128], BF16))
        gw_st = _es.enter_context(nc.sbuf_tensor("gw_st", [128, 16 * 16], F32))
        gw_b = _es.enter_context(nc.sbuf_tensor("gw_b", [128, 16 * 16], BF16))
        sel_sb = _es.enter_context(nc.sbuf_tensor("sel_sb", [16, 16 * 128], F32))
        cwT = _es.enter_context(nc.sbuf_tensor("cwT", [16, CH], F32))
        cw_sb = _es.enter_context(nc.sbuf_tensor("cw_sb", [128, 16], F32))
        sc_sb = _es.enter_context(nc.sbuf_tensor("sc_sb", [128, 16], F32))
        msk = _es.enter_context(nc.sbuf_tensor("msk", [128, 16], F32))
        mb = _es.enter_context(nc.sbuf_tensor("mb", [128, 16], F32))
        idf = _es.enter_context(nc.sbuf_tensor("idf", [128, 128], F32))
        t_sb = _es.enter_context(nc.sbuf_tensor("t_sb", [128, CH], F32))
        t2_sb = _es.enter_context(nc.sbuf_tensor("t2_sb", [128, CH], F32))
        xt_sb = _es.enter_context(nc.sbuf_tensor("xt_sb", [128, CH], F32))
        sf_sb = _es.enter_context(nc.sbuf_tensor("sf_sb", [128, CH], F32))
        sq_sb = _es.enter_context(nc.sbuf_tensor("sq_sb", [128, CH], F32))
        a_sb = _es.enter_context(nc.sbuf_tensor("a_sb", [128, CH], F32))
        m_sb = _es.enter_context(nc.sbuf_tensor("m_sb", [1, CH], F32))
        v_sb = _es.enter_context(nc.sbuf_tensor("v_sb", [1, CH], F32))
        r_sb = _es.enter_context(nc.sbuf_tensor("r_sb", [1, CH], F32))
        nmax = _es.enter_context(nc.sbuf_tensor("nmax", [128, 1], F32))
        den = _es.enter_context(nc.sbuf_tensor("den", [128, 1], F32))
        rec = _es.enter_context(nc.sbuf_tensor("rec", [128, 1], F32))
        mx = _es.enter_context(nc.sbuf_tensor("mx", [128, 1], F32))
        onesc = _es.enter_context(nc.sbuf_tensor("onesc", [128, 1], F32))
        ones1 = _es.enter_context(nc.sbuf_tensor("ones1", [1, 128], F32))
        epsap = _es.enter_context(nc.sbuf_tensor("epsap", [1, 1], F32))
        pp0 = _es.enter_context(nc.psum_tensor("pp0", [128, 512], F32))
        pp1 = _es.enter_context(nc.psum_tensor("pp1", [128, 512], F32))
        pp2 = _es.enter_context(nc.psum_tensor("pp2", [128, 512], F32))
        pp3 = _es.enter_context(nc.psum_tensor("pp3", [128, 512], F32))
        pp4 = _es.enter_context(nc.psum_tensor("pp4", [128, 512], F32))
        c = _es.enter_context(nc.semaphore("c"))
        block = _es.enter_context(nc.Block())
        pp = [pp0, pp1, pp2, pp4]
        S.op("vector", lambda e: e.memset(onesc[:], 1.0))
        S.op("vector", lambda e: e.memset(ones1[:], 1.0))
        S.op("sync", lambda e: e.dma_start(out=epsap[:], in_=epsd[:, :]), inc=16)
        S.op("sync", lambda e: e.dma_start(out=idf[:], in_=idd[:, :]), inc=16)
        S.op("sync", lambda e: e.dma_start(out=sel_sb[:], in_=seld[:, :]), inc=16)
        _loadw(S, gw_st, gwT, 0, 16, 16)
        S.op("scalar", lambda e: e.copy(out=gw_b[:], in_=gw_st[:]))
        lnbufs = (onesc, ones1, m_sb, v_sb, r_sb, sq_sb, a_sb, epsap)
        for tc in range(TPC // CH):
            tsl = slice(tc * CH, (tc + 1) * CH)
            for kt in range(16):
                S.op("sync", lambda e, kt=kt, tsl=tsl: e.dma_start(
                    out=xn_sb[:, kt * CH:(kt + 1) * CH],
                    in_=xnd[kt * 128:(kt + 1) * 128, tsl]), inc=16)
            for tt in range(CH // 128):
                for kt in range(16):
                    S.op("tensor", lambda e, kt=kt, tt=tt: e.matmul(
                        out=pp0[:, 0:16], lhsT=xn_sb[:, kt * CH + tt * 128: kt * CH + (tt + 1) * 128],
                        rhs=gw_b[:, kt * 16:(kt + 1) * 16], start=(kt == 0), stop=(kt == 15), skip_group_check=True),
                        seq=(kt == 0), inc=(1 if kt == 15 else 0))
                S.op("vector", lambda e: e.reduce_max(out=nmax[:], in_=pp0[:, 0:16], axis=AX.X, negate=True))
                S.op("scalar", lambda e: e.activation(out=sc_sb[:], in_=pp0[:, 0:16], func=AF.Exp, bias=nmax[:], scale=1.0, accum_out=den[:]))
                S.op("vector", lambda e: e.reciprocal(out=rec[:], in_=den[:]))
                S.op("vector", lambda e: e.tensor_scalar(out=sc_sb[:], in0=sc_sb[:], scalar1=rec[:], scalar2=None, op0=ALU.mult))
                S.op("vector", lambda e: e.memset(cw_sb[:], 0.0))
                for it in range(K0):
                    S.op("vector", lambda e: e.reduce_max(out=mx[:], in_=sc_sb[:], axis=AX.X))
                    S.op("vector", lambda e: e.tensor_scalar(out=msk[:], in0=sc_sb[:], scalar1=mx[:], scalar2=None, op0=ALU.is_equal))
                    S.op("vector", lambda e: e.tensor_tensor(out=mb[:], in0=msk[:], in1=sc_sb[:], op=ALU.mult))
                    S.op("vector", lambda e: e.tensor_tensor(out=cw_sb[:], in0=cw_sb[:], in1=mb[:], op=ALU.add))
                    S.op("vector", lambda e: e.tensor_scalar(out=msk[:], in0=msk[:], scalar1=1.0e9, scalar2=None, op0=ALU.mult))
                    S.op("vector", lambda e: e.tensor_tensor(out=sc_sb[:], in0=sc_sb[:], in1=msk[:], op=ALU.subtract))
                S.op("tensor", lambda e: e.transpose(out=pp1[0:16, 0:128], in_=cw_sb[:], identity=idf[:]))
                S.op("vector", lambda e, tt=tt: e.tensor_copy(out=cwT[0:16, tt * 128:(tt + 1) * 128], in_=pp1[0:16, 0:128]))
            S.op("vector", lambda e: e.memset(y_sb[:], 0.0))
            for ex in range(16):
                for m in range(11):
                    _loadw(S, wst, wgT[ex], m * 128, 128, 16)
                    S.op("scalar", lambda e: e.copy(out=wga[:], in_=wst[:]))
                    _loadw(S, wst, wuT[ex], m * 128, 128, 16)
                    S.op("scalar", lambda e: e.copy(out=wua[:], in_=wst[:]))
                    for kt in range(16):
                        S.op("tensor", lambda e, kt=kt: e.matmul(
                            out=pp0[:, 0:512], lhsT=wga[:, kt * 128:(kt + 1) * 128],
                            rhs=xn_sb[:, kt * CH:(kt + 1) * CH],
                            start=(kt == 0), stop=(kt == 15), skip_group_check=True),
                            seq=(kt == 0), inc=(1 if kt == 15 else 0))
                    for kt in range(16):
                        S.op("tensor", lambda e, kt=kt: e.matmul(
                            out=pp1[:, 0:512], lhsT=wua[:, kt * 128:(kt + 1) * 128],
                            rhs=xn_sb[:, kt * CH:(kt + 1) * CH],
                            start=(kt == 0), stop=(kt == 15), skip_group_check=True),
                            seq=(kt == 0), inc=(1 if kt == 15 else 0))
                    S.op("scalar", lambda e: e.activation(out=t_sb[:], in_=pp0[:, 0:512], func=AF.Silu))
                    S.op("vector", lambda e, m=m: e.tensor_tensor(
                        out=h_sb[:, m * CH:(m + 1) * CH], in0=t_sb[:], in1=pp1[:, 0:512], op=ALU.mult))
                S.op("tensor", lambda e, ex=ex: e.matmul(out=pp3[:, 0:512], lhsT=sel_sb[:, ex * 128:(ex + 1) * 128], rhs=cwT[0:16, :], start=True, stop=True, skip_group_check=True))
                for mo in range(16):
                    _loadw(S, wst, wdT[ex], mo * 128, 128, 11)
                    S.op("scalar", lambda e: e.copy(out=wda[:], in_=wst[:, 0:11 * 128]))
                    for kt in range(11):
                        S.op("tensor", lambda e, kt=kt: e.matmul(
                            out=pp2[:, 0:512], lhsT=wda[:, kt * 128:(kt + 1) * 128],
                            rhs=h_sb[:, kt * CH:(kt + 1) * CH],
                            start=(kt == 0), stop=(kt == 10), skip_group_check=True),
                            seq=(kt == 0), inc=(1 if kt == 10 else 0))
                    S.op("scalar", lambda e: e.copy(out=t_sb[:], in_=pp2[:, 0:512]))
                    S.op("vector", lambda e: e.tensor_tensor(out=t2_sb[:], in0=t_sb[:], in1=pp3[:, 0:512], op=ALU.mult))
                    S.op("vector", lambda e, mo=mo: e.tensor_tensor(
                        out=y_sb[:, mo * CH:(mo + 1) * CH], in0=y_sb[:, mo * CH:(mo + 1) * CH],
                        in1=t2_sb[:], op=ALU.add))
            for kt in range(16):
                S.op("sync", lambda e, kt=kt, tsl=tsl: e.dma_start(
                    out=xn_sb[:, kt * CH:(kt + 1) * CH],
                    in_=aod[kt * 128:(kt + 1) * 128, tsl]), inc=16)
            for mo in range(16):
                _loadw(S, wst, wo, mo * 128, 128, 16)
                S.op("scalar", lambda e: e.copy(out=wga[:], in_=wst[:]))
                for kt in range(16):
                    S.op("tensor", lambda e, kt=kt: e.matmul(
                        out=pp0[:, 0:512], lhsT=wga[:, kt * 128:(kt + 1) * 128],
                        rhs=xn_sb[:, kt * CH:(kt + 1) * CH],
                        start=(kt == 0), stop=(kt == 15), skip_group_check=True),
                        seq=(kt == 0), inc=(1 if kt == 15 else 0))
                S.op("scalar", lambda e, mo=mo: e.copy(out=sT_sb[:, mo * CH:(mo + 1) * CH], in_=pp0[:, 0:512]))
            for kt in range(16):
                S.op("sync", lambda e, kt=kt, tsl=tsl: e.dma_start(
                    out=xt_sb[:], in_=xTs[kt * 128:(kt + 1) * 128, tsl]), inc=16)
                S.op("scalar", lambda e, kt=kt: e.activation(
                    out=y_sb[:, kt * CH:(kt + 1) * CH], in_=y_sb[:, kt * CH:(kt + 1) * CH], func=AF.Silu))
                S.op("scalar", lambda e, kt=kt: e.copy(out=sf_sb[:], in_=sT_sb[:, kt * CH:(kt + 1) * CH]))
                S.op("vector", lambda e, kt=kt: e.tensor_tensor(
                    out=y_sb[:, kt * CH:(kt + 1) * CH], in0=y_sb[:, kt * CH:(kt + 1) * CH],
                    in1=sf_sb[:], op=ALU.mult))
                S.op("vector", lambda e, kt=kt: e.tensor_tensor(
                    out=y_sb[:, kt * CH:(kt + 1) * CH], in0=y_sb[:, kt * CH:(kt + 1) * CH],
                    in1=xt_sb[:], op=ALU.add))
                S.op("gpsimd", lambda e, kt=kt, tsl=tsl: e.dma_start(
                    out=x1T[kt * 128:(kt + 1) * 128, tsl], in_=y_sb[:, kt * CH:(kt + 1) * CH]), inc=16)
            emit_ln(S, lambda kt: y_sb[:, kt * CH:(kt + 1) * CH],
                    lambda kt: sT_sb[:, kt * CH:(kt + 1) * CH], pp, lnbufs)
            for kt in range(16):
                S.op("gpsimd", lambda e, kt=kt, tsl=tsl: e.dma_start(
                    out=xn1[kt * 128:(kt + 1) * 128, tsl], in_=sT_sb[:, kt * CH:(kt + 1) * CH]), inc=16)
        S.emit(block, c)
    return nc


def build_l4():
    nc = bass.Bass("TRN2", target_bir_lowering=False)
    xnd = nc.dram_tensor("xnd", [D, TPC], BF16, kind="ExternalInput").ap()
    Wq = nc.dram_tensor("Wq", [D, D], F32, kind="ExternalInput").ap()
    Wk = nc.dram_tensor("Wk", [D, D], F32, kind="ExternalInput").ap()
    Wv = nc.dram_tensor("Wv", [D, D], F32, kind="ExternalInput").ap()
    Wg1 = nc.dram_tensor("Wg1", [D, 128], F32, kind="ExternalInput").ap()
    Wg2 = nc.dram_tensor("Wg2", [128, D], F32, kind="ExternalInput").ap()
    ctd = nc.dram_tensor("ctd", [128, TPC], F32, kind="ExternalInput").ap()
    std = nc.dram_tensor("std", [128, TPC], F32, kind="ExternalInput").ap()
    cnd = nc.dram_tensor("cnd", [TPC, D], F32, kind="ExternalInput").ap()
    snd = nc.dram_tensor("snd", [TPC, D], F32, kind="ExternalInput").ap()
    qTb = nc.dram_tensor("qTb", [16, 128, TPC], BF16, kind="ExternalOutput").ap()
    kTb = nc.dram_tensor("kTb", [16, 128, TPC], BF16, kind="ExternalOutput").ap()
    kNb = nc.dram_tensor("kNb", [TPC, D], BF16, kind="ExternalOutput").ap()
    vNb = nc.dram_tensor("vNb", [TPC, D], BF16, kind="ExternalOutput").ap()
    gTb = nc.dram_tensor("gTb", [16, 128, TPC], BF16, kind="ExternalOutput").ap()
    S = Sched()
    with ExitStack() as _es:
        xn_sb = _es.enter_context(nc.sbuf_tensor("xn_sb", [128, 16 * TPC], BF16))
        wst = _es.enter_context(nc.sbuf_tensor("wst", [128, 16 * 128], F32))
        wb = _es.enter_context(nc.sbuf_tensor("wb", [128, 16 * 128], BF16))
        wst2 = _es.enter_context(nc.sbuf_tensor("wst2", [128, 16 * 512], F32))
        wb2 = _es.enter_context(nc.sbuf_tensor("wb2", [128, 16 * 512], BF16))
        cs = _es.enter_context(nc.sbuf_tensor("cs", [128, TPC], F32))
        sn = _es.enter_context(nc.sbuf_tensor("sn", [128, TPC], F32))
        cn_sb = _es.enter_context(nc.sbuf_tensor("cn_sb", [128, 512], F32))
        sn_sb = _es.enter_context(nc.sbuf_tensor("sn_sb", [128, 512], F32))
        a_sb = _es.enter_context(nc.sbuf_tensor("a_sb", [128, 512], F32))
        b_sb = _es.enter_context(nc.sbuf_tensor("b_sb", [128, 512], F32))
        qf = _es.enter_context(nc.sbuf_tensor("qf", [128, 512], F32))
        qo = _es.enter_context(nc.sbuf_tensor("qo", [128, 512], BF16))
        kn_f = _es.enter_context(nc.sbuf_tensor("kn_f", [128, 512], F32))
        kn_a = _es.enter_context(nc.sbuf_tensor("kn_a", [128, 512], F32))
        kn_b = _es.enter_context(nc.sbuf_tensor("kn_b", [128, 512], F32))
        kn_o = _es.enter_context(nc.sbuf_tensor("kn_o", [128, 512], BF16))
        gm_sb = _es.enter_context(nc.sbuf_tensor("gm_sb", [128, TPC], BF16))
        pp0 = _es.enter_context(nc.psum_tensor("pp0", [128, 512], F32))
        c = _es.enter_context(nc.semaphore("c"))
        block = _es.enter_context(nc.Block())
        S.op("sync", lambda e: e.dma_start(out=cs[:], in_=ctd[:, :]), inc=16)
        S.op("sync", lambda e: e.dma_start(out=sn[:], in_=std[:, :]), inc=16)
        for kt in range(16):
            S.op("sync", lambda e, kt=kt: e.dma_start(
                out=xn_sb[:, kt * TPC:(kt + 1) * TPC], in_=xnd[kt * 128:(kt + 1) * 128, :]), inc=16)

        def tproj(wsrc, outt, lrpe, silu):
            for m in range(16):
                _loadw(S, wst, wsrc, m * 128, 128, 16)
                S.op("scalar", lambda e: e.copy(out=wb[:], in_=wst[:]))
                for s4 in range(4):
                    for kt in range(16):
                        S.op("tensor", lambda e, kt=kt, s4=s4: e.matmul(
                            out=pp0[:, 0:512], lhsT=wb[:, kt * 128:(kt + 1) * 128],
                            rhs=xn_sb[:, kt * TPC + s4 * 512: kt * TPC + (s4 + 1) * 512],
                            start=(kt == 0), stop=(kt == 15), skip_group_check=True),
                            seq=(kt == 0), inc=(1 if kt == 15 else 0))
                    if silu:
                        S.op("scalar", lambda e: e.activation(out=qf[:], in_=pp0[:, 0:512], func=AF.Silu))
                    else:
                        S.op("scalar", lambda e: e.copy(out=qf[:], in_=pp0[:, 0:512]))
                    if lrpe:
                        S.op("vector", lambda e, s4=s4: e.tensor_tensor(out=a_sb[:], in0=qf[:], in1=cs[:, s4 * 512:(s4 + 1) * 512], op=ALU.mult))
                        S.op("vector", lambda e, s4=s4: e.tensor_tensor(out=b_sb[:], in0=qf[:], in1=sn[:, s4 * 512:(s4 + 1) * 512], op=ALU.mult))
                        S.op("scalar", lambda e: e.mul(out=qf[0:64, :], in_=b_sb[64:128, :], mul=-1.0))
                        S.op("scalar", lambda e: e.copy(out=qf[64:128, :], in_=b_sb[0:64, :]))
                        S.op("vector", lambda e: e.tensor_tensor(out=qf[:], in0=qf[:], in1=a_sb[:], op=ALU.add))
                    S.op("scalar", lambda e: e.copy(out=qo[:], in_=qf[:]))
                    S.op("gpsimd", lambda e, m=m, s4=s4: e.dma_start(
                        out=outt[m, :, s4 * 512:(s4 + 1) * 512], in_=qo[:]), inc=16)
        tproj(Wq, qTb, True, True)
        tproj(Wk, kTb, True, True)

        def nproj(wsrc, outt, lrpe, silu):
            for nck in range(4):
                for k0 in range(0, 16, 2):
                    S.op("sync", lambda e, k0=k0, nck=nck: e.dma_start(
                        out=wst2[:, k0 * 512:(k0 + 2) * 512].rearrange("p (k m) -> p k m", k=2),
                        in_=wsrc[k0 * 128:(k0 + 2) * 128, nck * 512:(nck + 1) * 512].rearrange("(k p) m -> p k m", p=128)), inc=16)
                S.op("scalar", lambda e: e.copy(out=wb2[:], in_=wst2[:]))
                for tt in range(16):
                    for kt in range(16):
                        S.op("tensor", lambda e, kt=kt, tt=tt: e.matmul(
                            out=pp0[:, 0:512], lhsT=xn_sb[:, kt * TPC + tt * 128: kt * TPC + (tt + 1) * 128],
                            rhs=wb2[:, kt * 512:(kt + 1) * 512],
                            start=(kt == 0), stop=(kt == 15), skip_group_check=True),
                            seq=(kt == 0), inc=(1 if kt == 15 else 0))
                    if silu:
                        S.op("scalar", lambda e: e.activation(out=kn_f[:], in_=pp0[:, 0:512], func=AF.Silu))
                    else:
                        S.op("scalar", lambda e: e.copy(out=kn_f[:], in_=pp0[:, 0:512]))
                    if lrpe:
                        S.op("sync", lambda e, tt=tt, nck=nck: e.dma_start(
                            out=cn_sb[:], in_=cnd[tt * 128:(tt + 1) * 128, nck * 512:(nck + 1) * 512]), inc=16)
                        S.op("sync", lambda e, tt=tt, nck=nck: e.dma_start(
                            out=sn_sb[:], in_=snd[tt * 128:(tt + 1) * 128, nck * 512:(nck + 1) * 512]), inc=16)
                        S.op("vector", lambda e: e.tensor_tensor(out=kn_a[:], in0=kn_f[:], in1=cn_sb[:], op=ALU.mult))
                        S.op("vector", lambda e: e.tensor_tensor(out=kn_b[:], in0=kn_f[:], in1=sn_sb[:], op=ALU.mult))
                        av = kn_a[:].rearrange("p (h two d) -> p h two d", two=2, d=64)
                        bv = kn_b[:].rearrange("p (h two d) -> p h two d", two=2, d=64)
                        ov = kn_o[:].rearrange("p (h two d) -> p h two d", two=2, d=64)
                        S.op("vector", lambda e, av=av, bv=bv, ov=ov: e.tensor_tensor(
                            out=ov[:, :, 0, :], in0=av[:, :, 0, :], in1=bv[:, :, 1, :], op=ALU.subtract))
                        S.op("vector", lambda e, av=av, bv=bv, ov=ov: e.tensor_tensor(
                            out=ov[:, :, 1, :], in0=bv[:, :, 0, :], in1=av[:, :, 1, :], op=ALU.add))
                    else:
                        S.op("vector", lambda e: e.tensor_copy(out=kn_o[:], in_=kn_f[:]))
                    S.op("gpsimd", lambda e, tt=tt, nck=nck: e.dma_start(
                        out=outt[tt * 128:(tt + 1) * 128, nck * 512:(nck + 1) * 512], in_=kn_o[:]), inc=16)
        nproj(Wk, kNb, True, True)
        nproj(Wv, vNb, False, False)
        _loadw(S, wst, Wg1, 0, 128, 16)
        S.op("scalar", lambda e: e.copy(out=wb[:], in_=wst[:]))
        for s4 in range(4):
            for kt in range(16):
                S.op("tensor", lambda e, kt=kt, s4=s4: e.matmul(
                    out=pp0[:, 0:512], lhsT=wb[:, kt * 128:(kt + 1) * 128],
                    rhs=xn_sb[:, kt * TPC + s4 * 512: kt * TPC + (s4 + 1) * 512],
                    start=(kt == 0), stop=(kt == 15), skip_group_check=True),
                    seq=(kt == 0), inc=(1 if kt == 15 else 0))
            S.op("scalar", lambda e, s4=s4: e.copy(out=gm_sb[:, s4 * 512:(s4 + 1) * 512], in_=pp0[:, 0:512]))
        for k0 in range(0, 16, 4):
            S.op("sync", lambda e, k0=k0: e.dma_start(
                out=wst2[:, k0 * 128:(k0 + 4) * 128].rearrange("p (k m) -> p k m", k=4),
                in_=Wg2[:, k0 * 128:(k0 + 4) * 128].rearrange("p (k m) -> p k m", k=4)), inc=16)
        S.op("scalar", lambda e: e.copy(out=wb2[:, 0:D], in_=wst2[:, 0:D]))
        for m in range(16):
            for s4 in range(4):
                S.op("tensor", lambda e, m=m, s4=s4: e.matmul(
                    out=pp0[:, 0:512], lhsT=wb2[:, m * 128:(m + 1) * 128],
                    rhs=gm_sb[:, s4 * 512:(s4 + 1) * 512], start=True, stop=True, skip_group_check=True))
                S.op("scalar", lambda e: e.copy(out=qo[:], in_=pp0[:, 0:512]))
                S.op("gpsimd", lambda e, m=m, s4=s4: e.dma_start(
                    out=gTb[m, :, s4 * 512:(s4 + 1) * 512], in_=qo[:]), inc=16)
        S.emit(block, c)
    return nc


def build_l5():
    nc = bass.Bass("TRN2", target_bir_lowering=False)
    qTd = nc.dram_tensor("qTd", [4, 128, N], BF16, kind="ExternalInput").ap()
    kTd = nc.dram_tensor("kTd", [4, 128, N], BF16, kind="ExternalInput").ap()
    kNd = nc.dram_tensor("kNd", [4, N, 128], BF16, kind="ExternalInput").ap()
    vNd = nc.dram_tensor("vNd", [4, N, 128], BF16, kind="ExternalInput").ap()
    gTd = nc.dram_tensor("gTd", [4, 128, N], BF16, kind="ExternalInput").ap()
    DTd = nc.dram_tensor("DTd", [4, 128, 128], F32, kind="ExternalInput").ap()
    dqd = nc.dram_tensor("dqd", [4, 128, 128], BF16, kind="ExternalInput").ap()
    dkd = nc.dram_tensor("dkd", [4, 128, 1], F32, kind="ExternalInput").ap()
    lcd = nc.dram_tensor("lcd", [4, 128, 1], F32, kind="ExternalInput").ap()
    nwd = nc.dram_tensor("nwd", [4, 128, 1], F32, kind="ExternalInput").ap()
    epsd = nc.dram_tensor("epsd", [1, 1], F32, kind="ExternalInput").ap()
    oTd = nc.dram_tensor("oTd", [4, 128, N], BF16, kind="ExternalOutput").ap()
    S = Sched()
    NCH = N // 128
    with ExitStack() as _es:
        qts = _es.enter_context(nc.sbuf_tensor("qts", [128, N], BF16))
        kts = _es.enter_context(nc.sbuf_tensor("kts", [128, N], BF16))
        kns = _es.enter_context(nc.sbuf_tensor("kns", [128, NCH * 128], BF16))
        vns = _es.enter_context(nc.sbuf_tensor("vns", [128, NCH * 128], BF16))
        gts = _es.enter_context(nc.sbuf_tensor("gts", [128, N], BF16))
        ot = _es.enter_context(nc.sbuf_tensor("ot", [128, N], BF16))
        DTt = _es.enter_context(nc.sbuf_tensor("DTt", [128, 128], F32))
        dqb = _es.enter_context(nc.sbuf_tensor("dqb", [128, 128], BF16))
        dkc = _es.enter_context(nc.sbuf_tensor("dkc", [128, 1], F32))
        lamc = _es.enter_context(nc.sbuf_tensor("lamc", [128, 1], F32))
        nw = _es.enter_context(nc.sbuf_tensor("nw", [128, 1], F32))
        Ssb = _es.enter_context(nc.sbuf_tensor("Ssb", [128, 128], BF16))
        St = _es.enter_context(nc.sbuf_tensor("St", [128, 128], BF16))
        PT = _es.enter_context(nc.sbuf_tensor("PT", [128, 128], BF16))
        qd = _es.enter_context(nc.sbuf_tensor("qd", [128, 128], BF16))
        kd = _es.enter_context(nc.sbuf_tensor("kd", [128, 128], BF16))
        gs = _es.enter_context(nc.sbuf_tensor("gs", [128, 128], F32))
        xg = _es.enter_context(nc.sbuf_tensor("xg", [128, 128], F32))
        xq = _es.enter_context(nc.sbuf_tensor("xq", [128, 128], F32))
        rs = _es.enter_context(nc.sbuf_tensor("rs", [1, 128], F32))
        rr = _es.enter_context(nc.sbuf_tensor("rr", [1, 128], F32))
        tt_sb = _es.enter_context(nc.sbuf_tensor("tt_sb", [128, 128], F32))
        onesc = _es.enter_context(nc.sbuf_tensor("onesc", [128, 1], F32))
        ones1 = _es.enter_context(nc.sbuf_tensor("ones1", [1, 128], F32))
        epsap = _es.enter_context(nc.sbuf_tensor("epsap", [1, 1], F32))
        ps_st = _es.enter_context(nc.psum_tensor("ps_st", [128, 128], F32))
        ps_o = _es.enter_context(nc.psum_tensor("ps_o", [128, 128], F32))
        ps_S = _es.enter_context(nc.psum_tensor("ps_S", [128, 128], F32))
        ps_ms = _es.enter_context(nc.psum_tensor("ps_ms", [1, 128], F32))
        ps_bc = _es.enter_context(nc.psum_tensor("ps_bc", [128, 128], F32))
        c = _es.enter_context(nc.semaphore("c"))
        block = _es.enter_context(nc.Block())
        S.op("vector", lambda e: e.memset(onesc[:], 1.0))
        S.op("vector", lambda e: e.memset(ones1[:], 1.0))
        S.op("sync", lambda e: e.dma_start(out=epsap[:], in_=epsd[:, :]), inc=16)
        for p in range(4):
            S.op("sync", lambda e, p=p: e.dma_start(out=qts[:], in_=qTd[p, :, :]), inc=16)
            S.op("sync", lambda e, p=p: e.dma_start(out=kts[:], in_=kTd[p, :, :]), inc=16)
            S.op("sync", lambda e, p=p: e.dma_start(out=gts[:], in_=gTd[p, :, :]), inc=16)
            for t0 in range(0, NCH, 8):
                S.op("sync", lambda e, p=p, t0=t0: e.dma_start(
                    out=kns[:, t0 * 128:(t0 + 8) * 128].rearrange("q (t d) -> q t d", d=128),
                    in_=kNd[p, t0 * 128:(t0 + 8) * 128, :].rearrange("(t q) d -> q t d", q=128)), inc=16)
                S.op("sync", lambda e, p=p, t0=t0: e.dma_start(
                    out=vns[:, t0 * 128:(t0 + 8) * 128].rearrange("q (t d) -> q t d", d=128),
                    in_=vNd[p, t0 * 128:(t0 + 8) * 128, :].rearrange("(t q) d -> q t d", q=128)), inc=16)
            S.op("sync", lambda e, p=p: e.dma_start(out=DTt[:], in_=DTd[p, :, :]), inc=16)
            S.op("sync", lambda e, p=p: e.dma_start(out=dqb[:], in_=dqd[p, :, :]), inc=16)
            S.op("sync", lambda e, p=p: e.dma_start(out=dkc[:], in_=dkd[p, :, :]), inc=16)
            S.op("sync", lambda e, p=p: e.dma_start(out=lamc[:], in_=lcd[p, :, :]), inc=16)
            S.op("sync", lambda e, p=p: e.dma_start(out=nw[:], in_=nwd[p, :, :]), inc=16)
            S.op("vector", lambda e: e.memset(Ssb[:], 0.0))
            for i in range(NCH):
                sl = slice(i * 128, (i + 1) * 128)
                S.op("tensor", lambda e, sl=sl: e.matmul(out=ps_st[:], lhsT=kts[:, sl], rhs=qts[:, sl], start=True, stop=True, skip_group_check=True))
                S.op("vector", lambda e: e.tensor_tensor(out=PT[:], in0=ps_st[:], in1=DTt[:], op=ALU.mult))
                S.op("vector", lambda e, sl=sl: e.tensor_tensor(out=qd[:], in0=qts[:, sl], in1=dqb[:], op=ALU.mult))
                S.op("vector", lambda e, sl=sl: e.tensor_scalar(out=kd[:], in0=kns[:, sl], scalar1=dkc[:], scalar2=None, op0=ALU.mult))
                S.op("tensor", lambda e, sl=sl: e.matmul(out=ps_o[:], lhsT=vns[:, sl], rhs=PT[:], start=True, stop=False, skip_group_check=True), inc=0)
                S.op("tensor", lambda e: e.matmul(out=ps_o[:], lhsT=Ssb[:], rhs=qd[:], start=False, stop=True, skip_group_check=True))
                S.op("tensor", lambda e, sl=sl: e.matmul(out=ps_S[:], lhsT=kd[:], rhs=vns[:, sl], start=True, stop=True, skip_group_check=True))
                S.op("vector", lambda e: e.tensor_scalar(out=St[:], in0=Ssb[:], scalar1=lamc[:], scalar2=None, op0=ALU.mult))
                S.op("vector", lambda e: e.tensor_tensor(out=Ssb[:], in0=St[:], in1=ps_S[:], op=ALU.add))
                S.op("scalar", lambda e, sl=sl: e.activation(out=gs[:], in_=gts[:, sl], func=AF.Sigmoid))
                S.op("vector", lambda e: e.tensor_tensor(out=xg[:], in0=ps_o[:], in1=gs[:], op=ALU.mult))
                S.op("scalar", lambda e: e.activation(out=xq[:], in_=xg[:], func=AF.Square))
                S.op("tensor", lambda e: e.matmul(out=ps_ms[0:1, :], lhsT=onesc[:], rhs=xq[:], start=True, stop=True, skip_group_check=True))
                S.op("scalar", lambda e: e.activation(out=rs[0:1, :], in_=ps_ms[0:1, :], func=AF.Sqrt, bias=epsap[0:1, 0:1], scale=1.0 / 128.0))
                S.op("vector", lambda e: e.reciprocal(out=rr[0:1, :], in_=rs[0:1, :]))
                S.op("tensor", lambda e: e.matmul(out=ps_bc[:], lhsT=ones1[:], rhs=rr[0:1, :], start=True, stop=True, skip_group_check=True))
                S.op("vector", lambda e: e.tensor_tensor(out=tt_sb[:], in0=xg[:], in1=ps_bc[:], op=ALU.mult))
                S.op("vector", lambda e, sl=sl: e.tensor_scalar(out=ot[:, sl], in0=tt_sb[:], scalar1=nw[:], scalar2=None, op0=ALU.mult))
            S.op("gpsimd", lambda e, p=p: e.dma_start(out=oTd[p, :, :], in_=ot[:]), inc=16)
        S.emit(block, c)
    return nc


def build_l6():
    nc = bass.Bass("TRN2", target_bir_lowering=False)
    xnd = nc.dram_tensor("xnd", [D, TPC], BF16, kind="ExternalInput").ap()
    x1d = nc.dram_tensor("x1d", [D, TPC], F32, kind="ExternalInput").ap()
    ond = nc.dram_tensor("ond", [D, TPC], BF16, kind="ExternalInput").ap()
    Wo = nc.dram_tensor("Wo", [D, D], F32, kind="ExternalInput").ap()
    Wr = nc.dram_tensor("Wr", [D, 8], F32, kind="ExternalInput").ap()
    We = nc.dram_tensor("We", [8, D, D], F32, kind="ExternalInput").ap()
    beT = nc.dram_tensor("beT", [D, 8], F32, kind="ExternalInput").ap()
    sel8d = nc.dram_tensor("sel8d", [8, 8 * 128], F32, kind="ExternalInput").ap()
    idd = nc.dram_tensor("idd", [128, 128], F32, kind="ExternalInput").ap()
    OUT = nc.dram_tensor("OUT", [TPC, D], F32, kind="ExternalOutput").ap()
    S = Sched()
    with ExitStack() as _es:
        xn_sb = _es.enter_context(nc.sbuf_tensor("xn_sb", [128, 16 * 512], BF16))
        on_sb = _es.enter_context(nc.sbuf_tensor("on_sb", [128, 16 * 512], BF16))
        s1_sb = _es.enter_context(nc.sbuf_tensor("s1_sb", [128, 16 * 512], BF16))
        y2_sb = _es.enter_context(nc.sbuf_tensor("y2_sb", [128, 16 * 512], F32))
        wo_b = _es.enter_context(nc.sbuf_tensor("wo_b", [128, 16 * D], BF16))
        wst = _es.enter_context(nc.sbuf_tensor("wst", [128, 16 * 128], F32))
        web = _es.enter_context(nc.sbuf_tensor("web", [128, 16 * 128], BF16))
        wr_st = _es.enter_context(nc.sbuf_tensor("wr_st", [128, 16 * 8], F32))
        wr_b = _es.enter_context(nc.sbuf_tensor("wr_b", [128, 16 * 8], BF16))
        be_sb = _es.enter_context(nc.sbuf_tensor("be_sb", [128, 16 * 8], F32))
        sel8 = _es.enter_context(nc.sbuf_tensor("sel8", [8, 8 * 128], F32))
        geT = _es.enter_context(nc.sbuf_tensor("geT", [8, 512], F32))
        idf = _es.enter_context(nc.sbuf_tensor("idf", [128, 128], F32))
        lg = _es.enter_context(nc.sbuf_tensor("lg", [128, 8], F32))
        m1 = _es.enter_context(nc.sbuf_tensor("m1", [128, 1], F32))
        m2 = _es.enter_context(nc.sbuf_tensor("m2", [128, 1], F32))
        msk1 = _es.enter_context(nc.sbuf_tensor("msk1", [128, 8], F32))
        msk2 = _es.enter_context(nc.sbuf_tensor("msk2", [128, 8], F32))
        mbig = _es.enter_context(nc.sbuf_tensor("mbig", [128, 8], F32))
        d21 = _es.enter_context(nc.sbuf_tensor("d21", [128, 1], F32))
        e2 = _es.enter_context(nc.sbuf_tensor("e2", [128, 1], F32))
        den = _es.enter_context(nc.sbuf_tensor("den", [128, 1], F32))
        w1 = _es.enter_context(nc.sbuf_tensor("w1", [128, 1], F32))
        w2 = _es.enter_context(nc.sbuf_tensor("w2", [128, 1], F32))
        gt = _es.enter_context(nc.sbuf_tensor("gt", [128, 8], F32))
        gt2 = _es.enter_context(nc.sbuf_tensor("gt2", [128, 8], F32))
        t_sb = _es.enter_context(nc.sbuf_tensor("t_sb", [128, 512], F32))
        t2_sb = _es.enter_context(nc.sbuf_tensor("t2_sb", [128, 512], F32))
        sf_sb = _es.enter_context(nc.sbuf_tensor("sf_sb", [128, 512], F32))
        xt_sb = _es.enter_context(nc.sbuf_tensor("xt_sb", [128, 512], F32))
        of_sb = _es.enter_context(nc.sbuf_tensor("of_sb", [128, 512], F32))
        ot_sb = _es.enter_context(nc.sbuf_tensor("ot_sb", [128, 128], F32))
        pp0 = _es.enter_context(nc.psum_tensor("pp0", [128, 512], F32))
        pp3 = _es.enter_context(nc.psum_tensor("pp3", [128, 512], F32))
        pp4 = _es.enter_context(nc.psum_tensor("pp4", [128, 512], F32))
        pp5 = _es.enter_context(nc.psum_tensor("pp5", [128, 512], F32))
        c = _es.enter_context(nc.semaphore("c"))
        block = _es.enter_context(nc.Block())
        S.op("sync", lambda e: e.dma_start(out=idf[:], in_=idd[:, :]), inc=16)
        S.op("sync", lambda e: e.dma_start(out=sel8[:], in_=sel8d[:, :]), inc=16)
        _loadw(S, wr_st, Wr, 0, 8, 16)
        S.op("scalar", lambda e: e.copy(out=wr_b[:], in_=wr_st[:]))
        _loadw(S, be_sb, beT, 0, 8, 16)
        for mo in range(16):
            _loadw(S, wst, Wo, mo * 128, 128, 16)
            S.op("scalar", lambda e, mo=mo: e.copy(out=wo_b[:, mo * D:(mo + 1) * D], in_=wst[:]))
        for tc in range(4):
            tsl = slice(tc * 512, (tc + 1) * 512)
            for kt in range(16):
                S.op("sync", lambda e, kt=kt, tsl=tsl: e.dma_start(
                    out=xn_sb[:, kt * 512:(kt + 1) * 512],
                    in_=xnd[kt * 128:(kt + 1) * 128, tsl]), inc=16)
                S.op("sync", lambda e, kt=kt, tsl=tsl: e.dma_start(
                    out=on_sb[:, kt * 512:(kt + 1) * 512],
                    in_=ond[kt * 128:(kt + 1) * 128, tsl]), inc=16)
            for tt in range(4):
                for kt in range(16):
                    S.op("tensor", lambda e, kt=kt, tt=tt: e.matmul(
                        out=pp0[:, 0:8], lhsT=xn_sb[:, kt * 512 + tt * 128: kt * 512 + (tt + 1) * 128],
                        rhs=wr_b[:, kt * 8:(kt + 1) * 8], start=(kt == 0), stop=(kt == 15), skip_group_check=True),
                        seq=(kt == 0), inc=(1 if kt == 15 else 0))
                S.op("vector", lambda e: e.tensor_copy(out=lg[:], in_=pp0[:, 0:8]))
                S.op("vector", lambda e: e.reduce_max(out=m1[:], in_=lg[:], axis=AX.X))
                S.op("vector", lambda e: e.tensor_scalar(out=msk1[:], in0=lg[:], scalar1=m1[:], scalar2=None, op0=ALU.is_equal))
                S.op("vector", lambda e: e.tensor_scalar(out=mbig[:], in0=msk1[:], scalar1=1.0e9, scalar2=None, op0=ALU.mult))
                S.op("vector", lambda e: e.tensor_tensor(out=mbig[:], in0=lg[:], in1=mbig[:], op=ALU.subtract))
                S.op("vector", lambda e: e.reduce_max(out=m2[:], in_=mbig[:], axis=AX.X))
                S.op("vector", lambda e: e.tensor_scalar(out=msk2[:], in0=mbig[:], scalar1=m2[:], scalar2=None, op0=ALU.is_equal))
                S.op("vector", lambda e: e.tensor_tensor(out=d21[:], in0=m2[:], in1=m1[:], op=ALU.subtract))
                S.op("scalar", lambda e: e.activation(out=e2[:], in_=d21[:], func=AF.Exp))
                S.op("vector", lambda e: e.tensor_scalar(out=den[:], in0=e2[:], scalar1=1.0, scalar2=None, op0=ALU.add))
                S.op("vector", lambda e: e.reciprocal(out=w1[:], in_=den[:]))
                S.op("vector", lambda e: e.tensor_tensor(out=w2[:], in0=e2[:], in1=w1[:], op=ALU.mult))
                S.op("vector", lambda e: e.tensor_scalar(out=gt[:], in0=msk1[:], scalar1=w1[:], scalar2=None, op0=ALU.mult))
                S.op("vector", lambda e: e.tensor_scalar(out=gt2[:], in0=msk2[:], scalar1=w2[:], scalar2=None, op0=ALU.mult))
                S.op("vector", lambda e: e.tensor_tensor(out=gt[:], in0=gt[:], in1=gt2[:], op=ALU.add))
                S.op("tensor", lambda e: e.transpose(out=pp5[0:8, 0:128], in_=gt[:], identity=idf[:]))
                S.op("vector", lambda e, tt=tt: e.tensor_copy(out=geT[0:8, tt * 128:(tt + 1) * 128], in_=pp5[0:8, 0:128]))
            for mo in range(16):
                for kt in range(16):
                    S.op("tensor", lambda e, kt=kt, mo=mo: e.matmul(
                        out=pp0[:, 0:512], lhsT=wo_b[:, mo * D + kt * 128: mo * D + (kt + 1) * 128],
                        rhs=on_sb[:, kt * 512:(kt + 1) * 512],
                        start=(kt == 0), stop=(kt == 15), skip_group_check=True),
                        seq=(kt == 0), inc=(1 if kt == 15 else 0))
                S.op("scalar", lambda e, mo=mo: e.copy(out=s1_sb[:, mo * 512:(mo + 1) * 512], in_=pp0[:, 0:512]))
            S.op("vector", lambda e: e.memset(y2_sb[:], 0.0))
            for ex in range(8):
                S.op("tensor", lambda e, ex=ex: e.matmul(out=pp3[:, 0:512], lhsT=sel8[:, ex * 128:(ex + 1) * 128], rhs=geT[0:8, :], start=True, stop=True, skip_group_check=True))
                for mo in range(16):
                    _loadw(S, wst, We[ex], mo * 128, 128, 16)
                    S.op("scalar", lambda e: e.copy(out=web[:], in_=wst[:]))
                    for kt in range(16):
                        S.op("tensor", lambda e, kt=kt: e.matmul(
                            out=pp0[:, 0:512], lhsT=web[:, kt * 128:(kt + 1) * 128],
                            rhs=xn_sb[:, kt * 512:(kt + 1) * 512],
                            start=(kt == 0), stop=(kt == 15), skip_group_check=True),
                            seq=(kt == 0), inc=(1 if kt == 15 else 0))
                    S.op("vector", lambda e, mo=mo, ex=ex: e.tensor_scalar(
                        out=t_sb[:], in0=pp0[:, 0:512], scalar1=be_sb[:, mo * 8 + ex: mo * 8 + ex + 1], scalar2=None, op0=ALU.add))
                    S.op("vector", lambda e: e.tensor_tensor(out=t2_sb[:], in0=t_sb[:], in1=pp3[:, 0:512], op=ALU.mult))
                    S.op("vector", lambda e, mo=mo: e.tensor_tensor(
                        out=y2_sb[:, mo * 512:(mo + 1) * 512], in0=y2_sb[:, mo * 512:(mo + 1) * 512],
                        in1=t2_sb[:], op=ALU.add))
            for mo in range(16):
                S.op("scalar", lambda e, mo=mo: e.activation(out=y2_sb[:, mo * 512:(mo + 1) * 512], in_=y2_sb[:, mo * 512:(mo + 1) * 512], func=AF.Silu))
                S.op("scalar", lambda e, mo=mo: e.copy(out=sf_sb[:], in_=s1_sb[:, mo * 512:(mo + 1) * 512]))
                S.op("sync", lambda e, mo=mo, tsl=tsl: e.dma_start(
                    out=xt_sb[:], in_=x1d[mo * 128:(mo + 1) * 128, tsl]), inc=16)
                S.op("vector", lambda e, mo=mo: e.tensor_tensor(out=of_sb[:], in0=y2_sb[:, mo * 512:(mo + 1) * 512], in1=sf_sb[:], op=ALU.mult))
                S.op("vector", lambda e: e.tensor_tensor(out=of_sb[:], in0=of_sb[:], in1=xt_sb[:], op=ALU.add))
                for j in range(4):
                    S.op("tensor", lambda e, j=j: e.transpose(out=pp4[:, 0:128], in_=of_sb[:, j * 128:(j + 1) * 128], identity=idf[:]))
                    S.op("vector", lambda e: e.tensor_copy(out=ot_sb[:], in_=pp4[:, 0:128]))
                    S.op("gpsimd", lambda e, j=j, mo=mo, tc=tc: e.dma_start(
                        out=OUT[(tc * 4 + j) * 128:(tc * 4 + j + 1) * 128, mo * 128:(mo + 1) * 128], in_=ot_sb[:]), inc=16)
        S.emit(block, c)
    return nc


def kernel(**inp):
    f32 = np.float32
    x = np.asarray(inp["L0_attn_x"], dtype=f32)
    x2 = x.reshape(T, D)
    xT = np.ascontiguousarray(x2.T)

    t_ar = np.arange(N, dtype=np.float64)
    freqs = 1.0 / (10000.0 ** (np.arange(0, HD, 2, dtype=np.float64) / HD))
    ang = t_ar[:, None] * freqs[None, :]
    c64 = np.cos(ang); s64 = np.sin(ang)
    cosT0 = np.ascontiguousarray(np.vstack([c64.T, c64.T]).astype(f32))
    sinT0 = np.ascontiguousarray(np.vstack([s64.T, s64.T]).astype(f32))
    CN = np.ascontiguousarray(np.tile(np.concatenate([c64, c64], 1), (1, 16)).astype(f32))
    SN = np.ascontiguousarray(np.tile(np.concatenate([s64, s64], 1), (1, 16)).astype(f32))
    mask0 = (np.triu(np.ones((128, 128)), k=1) * NEG).astype(f32)
    idf = np.eye(128, dtype=f32)
    idb = np.eye(128).astype(BF)
    epsln = np.array([[EPS_LN]], dtype=f32)
    epsgn = np.array([[EPS_GN]], dtype=f32)
    sel16 = np.zeros((16, 16 * 128), dtype=f32)
    for e in range(16):
        sel16[e, e * 128:(e + 1) * 128] = 1.0
    sel8 = np.zeros((8, 8 * 128), dtype=f32)
    for e in range(8):
        sel8[e, e * 128:(e + 1) * 128] = 1.0

    nc1 = build_l1()
    maps1 = []
    for cidx in range(NCORE):
        pos0 = (cidx % 4) * TPC
        maps1.append({
            "xTs": np.ascontiguousarray(xT[:, cidx * TPC:(cidx + 1) * TPC]),
            "wq": np.asarray(inp["L0_attn_wq"], f32),
            "wk": np.asarray(inp["L0_attn_wk"], f32),
            "wv": np.asarray(inp["L0_attn_wv"], f32),
            "cosT": np.ascontiguousarray(cosT0[:, pos0:pos0 + TPC]),
            "sinT": np.ascontiguousarray(sinT0[:, pos0:pos0 + TPC]),
            "epsd": epsln,
        })
    r1 = run_bass_kernel_spmd(nc1, maps1, core_ids=list(range(NCORE)))
    xnb = [r1.results[i]["xnb"] for i in range(NCORE)]
    qTs = [r1.results[i]["qT"] for i in range(NCORE)]
    kTs = [r1.results[i]["kT"] for i in range(NCORE)]
    vTs = [r1.results[i]["vT"] for i in range(NCORE)]

    nc2 = build_l2()
    maps2 = []
    for cidx in range(NCORE):
        b = cidx // 4
        kvh = cidx % 4
        cores_b = range(4 * b, 4 * b + 4)
        qT4 = np.concatenate([qTs[cc][4 * kvh:4 * kvh + 4] for cc in cores_b], axis=2)
        kTh = np.concatenate([kTs[cc][kvh] for cc in cores_b], axis=1)
        vTh = np.concatenate([vTs[cc][kvh] for cc in cores_b], axis=1)
        vN = np.ascontiguousarray(vTh.T)
        maps2.append({"qTd": np.ascontiguousarray(qT4), "kTd": np.ascontiguousarray(kTh),
                      "vNd": vN, "mkd": mask0, "idd": idb})
    r2 = run_bass_kernel_spmd(nc2, maps2, core_ids=list(range(NCORE)))
    aoT = np.empty((D, T), dtype=BF)
    for cidx in range(NCORE):
        b = cidx // 4
        kvh = cidx % 4
        ao = r2.results[cidx]["aoT"]
        for qq in range(4):
            h = 4 * kvh + qq
            aoT[h * 128:(h + 1) * 128, b * N:(b + 1) * N] = ao[qq]

    nc3 = build_l3()
    wgT = np.ascontiguousarray(np.asarray(inp["L0_ffn_w_gate"], f32).transpose(0, 2, 1))
    wuT = np.ascontiguousarray(np.asarray(inp["L0_ffn_w_up"], f32).transpose(0, 2, 1))
    wdT = np.ascontiguousarray(np.asarray(inp["L0_ffn_w_down"], f32).transpose(0, 2, 1))
    gwT = np.ascontiguousarray(np.asarray(inp["L0_ffn_gate_w"], f32).T)
    maps3 = []
    for cidx in range(NCORE):
        maps3.append({
            "xTs": np.ascontiguousarray(xT[:, cidx * TPC:(cidx + 1) * TPC]),
            "xnd": xnb[cidx],
            "aod": np.ascontiguousarray(aoT[:, cidx * TPC:(cidx + 1) * TPC]),
            "wo": np.asarray(inp["L0_attn_wo"], f32),
            "gwT": gwT, "wgT": wgT, "wuT": wuT, "wdT": wdT,
            "seld": sel16, "idd": idf, "epsd": epsln,
        })
    r3 = run_bass_kernel_spmd(nc3, maps3, core_ids=list(range(NCORE)))
    x1Ts = [r3.results[i]["x1T"] for i in range(NCORE)]
    xn1s = [r3.results[i]["xn1"] for i in range(NCORE)]

    nc4 = build_l4()
    maps4 = []
    for cidx in range(NCORE):
        pos0 = (cidx % 4) * TPC
        maps4.append({
            "xnd": xn1s[cidx],
            "Wq": np.asarray(inp["L1_attn_Wq"], f32), "Wk": np.asarray(inp["L1_attn_Wk"], f32),
            "Wv": np.asarray(inp["L1_attn_Wv"], f32),
            "Wg1": np.asarray(inp["L1_attn_Wg1"], f32), "Wg2": np.asarray(inp["L1_attn_Wg2"], f32),
            "ctd": np.ascontiguousarray(cosT0[:, pos0:pos0 + TPC]),
            "std": np.ascontiguousarray(sinT0[:, pos0:pos0 + TPC]),
            "cnd": np.ascontiguousarray(CN[pos0:pos0 + TPC, :]),
            "snd": np.ascontiguousarray(SN[pos0:pos0 + TPC, :]),
        })
    r4 = run_bass_kernel_spmd(nc4, maps4, core_ids=list(range(NCORE)))

    slopes = 2.0 ** (-8.0 * (np.arange(16, dtype=np.float64) + 1.0) / 16.0)
    log_lam = -slopes
    i_ar = np.arange(128, dtype=np.float64)
    nw_full = np.asarray(inp["L1_attn_norm_weight"], f32)
    nc5 = build_l5()
    maps5 = []
    for cidx in range(NCORE):
        pr = [divmod(4 * cidx + j, 16) for j in range(4)]
        qTl, kTl, kNl, vNl, gTl, DTl, dql, dkl, lcl, nwl = [], [], [], [], [], [], [], [], [], []
        for (b, h) in pr:
            cores_b = range(4 * b, 4 * b + 4)
            qTl.append(np.concatenate([r4.results[cc]["qTb"][h] for cc in cores_b], axis=1))
            kTl.append(np.concatenate([r4.results[cc]["kTb"][h] for cc in cores_b], axis=1))
            kNl.append(np.concatenate([r4.results[cc]["kNb"][:, h * 128:(h + 1) * 128] for cc in cores_b], axis=0))
            vNl.append(np.concatenate([r4.results[cc]["vNb"][:, h * 128:(h + 1) * 128] for cc in cores_b], axis=0))
            gTl.append(np.concatenate([r4.results[cc]["gTb"][h] for cc in cores_b], axis=1))
            ll = log_lam[h]
            diff = i_ar[:, None] - i_ar[None, :]
            Dm = np.where(diff >= 0, np.exp(ll * diff), 0.0)
            DTl.append(np.ascontiguousarray(Dm.T).astype(f32))
            dq = np.exp(ll * (i_ar + 1.0))
            dql.append(np.tile(dq[None, :], (128, 1)).astype(BF))
            dk = np.exp(ll * (127.0 - i_ar))
            dkl.append(np.ascontiguousarray(dk[:, None]).astype(f32))
            lcl.append(np.full((128, 1), np.exp(ll * 128.0), dtype=f32))
            nwl.append(np.ascontiguousarray(nw_full[h * 128:(h + 1) * 128][:, None]))
        maps5.append({
            "qTd": np.ascontiguousarray(np.stack(qTl)), "kTd": np.ascontiguousarray(np.stack(kTl)),
            "kNd": np.ascontiguousarray(np.stack(kNl)), "vNd": np.ascontiguousarray(np.stack(vNl)),
            "gTd": np.ascontiguousarray(np.stack(gTl)),
            "DTd": np.stack(DTl), "dqd": np.stack(dql), "dkd": np.stack(dkl),
            "lcd": np.stack(lcl), "nwd": np.stack(nwl),
            "epsd": epsgn,
        })
    r5 = run_bass_kernel_spmd(nc5, maps5, core_ids=list(range(NCORE)))
    onT = np.empty((D, T), dtype=BF)
    for cidx in range(NCORE):
        for j in range(4):
            b, h = divmod(4 * cidx + j, 16)
            onT[h * 128:(h + 1) * 128, b * N:(b + 1) * N] = r5.results[cidx]["oTd"][j]

    nc6 = build_l6()
    beT = np.ascontiguousarray(np.asarray(inp["L1_ffn_be"], f32).T)
    maps6 = []
    for cidx in range(NCORE):
        maps6.append({
            "xnd": xn1s[cidx], "x1d": x1Ts[cidx],
            "ond": np.ascontiguousarray(onT[:, cidx * TPC:(cidx + 1) * TPC]),
            "Wo": np.asarray(inp["L1_attn_Wo"], f32),
            "Wr": np.asarray(inp["L1_ffn_Wr"], f32),
            "We": np.asarray(inp["L1_ffn_We"], f32),
            "beT": beT, "sel8d": sel8, "idd": idf,
        })
    r6 = run_bass_kernel_spmd(nc6, maps6, core_ids=list(range(NCORE)))
    out = np.concatenate([r6.results[i]["OUT"] for i in range(NCORE)], axis=0)
    return np.ascontiguousarray(out.reshape(B, N, D).astype(np.float32))
